# version 01
# baseline (speedup 1.0000x reference)
import numpy as np
import ml_dtypes
import concourse.bass as bass
import concourse.mybir as mybir
from concourse.bass_utils import run_bass_kernel_spmd
from contextlib import ExitStack

F32 = mybir.dt.float32
BF16 = mybir.dt.bfloat16
FP8 = mybir.dt.float8e4
ALU = mybir.AluOpType
ACT = mybir.ActivationFunctionType
AX = mybir.AxisListType

D = 2048
NB = 2
T = 4096
NT = 8
NDT = 16
NH = 16
ISZ = 8192
SCALE_ATT = float(1.0 / np.sqrt(128.0))
QSC = 1.0 / 224.0
QEPS = 448e-30

PHASES = 4
SKIP_FFN = False
GRP4 = [[0, 1, 2, 3], [4, 5, 6, 7]]
GRP2 = [[0, 4], [1, 5], [2, 6], [3, 7]]


class Ctx:

    def __init__(self, nc, st):
        self.nc = nc
        self.block = st.enter_context(nc.Block())
        self.h = {}
        self.n = {}
        for s in ["sd", "sg", "sv", "ss", "stt", "scc"]:
            self.h[s] = st.enter_context(nc.semaphore(s))
            self.n[s] = 0

    def inc(self, instr, name, k=1):
        instr.then_inc(self.h[name], k)
        self.n[name] += k
        return self.n[name]

    def wait(self, eng, name, val=None):
        eng.wait_ge(self.h[name], self.n[name] if val is None else val)


class TrPipe:

    def __init__(self, cx, pa, pb, ident, ev_sem):
        self.cx = cx
        self.pa, self.pb = pa, pb
        self.ident = ident
        self.ev_sem = ev_sem

    def group(self, in_aps, emit_evict, ev_eng_attr, pre_waits=None, per_tr_wait=None):
        cx = self.cx
        n = len(in_aps)
        tt0 = cx.n["stt"]
        ev0 = cx.n[self.ev_sem]

        @cx.block.tensor
        def _(te):
            for i, ap in enumerate(in_aps):
                if i == 0 and pre_waits:
                    for sem, val in pre_waits:
                        te.wait_ge(cx.h[sem], val)
                if per_tr_wait is not None:
                    w = per_tr_wait(i)
                    if w is not None:
                        te.wait_ge(cx.h[w[0]], w[1])
                wv = ev0 + max(0, i - 1)
                if wv > 0:
                    te.wait_ge(cx.h[self.ev_sem], wv)
                ps = self.pa if i % 2 == 0 else self.pb
                cx.inc(te.transpose(out=ps[:], in_=ap, identity=self.ident[:]), "stt")

        eng_dec = getattr(cx.block, ev_eng_attr)

        @eng_dec
        def _(eng):
            for i in range(n):
                eng.wait_ge(cx.h["stt"], tt0 + i + 1)
                instr = emit_evict(eng, i, (self.pa if i % 2 == 0 else self.pb)[:])
                cx.inc(instr, self.ev_sem)


def build(phases=PHASES):
    nc = bass.Bass("TRN2", target_bir_lowering=False)

    def din(name, shape, dtype=F32):
        return nc.dram_tensor(name, shape, dtype, kind="ExternalInput").ap()

    def dint(name, shape, dtype):
        return nc.dram_tensor(name, shape, dtype, kind="Internal").ap()

    x8 = din("x8", [NT, 128, D])
    wa_sl = din("wa_sl", [D, 256])
    b_col = din("b_col", [128, NDT])
    gate_sl = din("gate_sl", [ISZ, 256])
    up_sl = din("up_sl", [ISZ, 256])
    down_sl = din("down_sl", [D, 1024])
    wdkv = din("wdkv", [D, 512])
    wuk = din("wuk", [512, D])
    wuv = din("wuv", [512, D])
    wdq = din("wdq", [D, 1024])
    wuq = din("wuq", [1024, 4096])
    wlam = din("wlam", [D, NH])
    wout = din("wout", [D, D])
    blam_rep = din("blam_rep", [128, NH])
    cos_t = din("cos_t", [128, 1024])
    sinp_t = din("sinp_t", [128, 1024])
    mask8 = din("mask8", [128, 8 * 128], BF16)
    sel4 = din("sel4", [128, 4 * 32])
    id_bf = din("id_bf", [128, 128], BF16)
    id_f32 = din("id_f32", [128, 128])
    out8 = nc.dram_tensor("out8", [NT, 128, D], F32, kind="ExternalOutput").ap()
    dbg = nc.dram_tensor("dbg", [4, 128, 1024], F32, kind="ExternalOutput").ap()

    ccA_i = dint("ccA_i", [256, D], BF16)
    ccA_m = dint("ccA_m", [1024, D], BF16)
    aWT = dint("aWT", [D, D], BF16)
    ccG_i = dint("ccG_i", [256, ISZ], BF16)
    ccG_m = dint("ccG_m", [1024, ISZ], BF16)
    gateT = dint("gateT", [D, ISZ], BF16)
    ccU_i = dint("ccU_i", [256, ISZ], BF16)
    ccU_m = dint("ccU_m", [1024, ISZ], BF16)
    upT = dint("upT", [D, ISZ], BF16)
    ccD_i = dint("ccD_i", [1024, D], BF16)
    ccD_m = dint("ccD_m", [4096, D], BF16)
    downT = dint("downT", [ISZ, D], BF16)
    ccS_i = dint("ccS_i", [128, 256], F32)
    ccS_m = dint("ccS_m", [512, 256], F32)
    ccS_o = dint("ccS_o", [1024, 256], F32)
    ccK_i = dint("ccK_i", [2048, 1024], BF16)
    ccK_m = dint("ccK_m", [8192, 1024], BF16)
    K_g = dint("K_g", [16384, 1024], BF16)
    ccV_i = dint("ccV_i", [1024, 2080], BF16)
    ccV_m = dint("ccV_m", [4096, 2080], BF16)
    V_g = dint("V_g", [8192, 2080], BF16)
    q_loc = dint("q_loc", [4096, 1024], BF16)

    with ExitStack() as st:
        cx = Ctx(nc, st)
        block = cx.block

        def sb(stk, name, shape, dtype):
            return stk.enter_context(nc.sbuf_tensor(name, shape, dtype))

        def ps(stk, name, shape, dtype=F32):
            return stk.enter_context(nc.psum_tensor(name, shape, dtype))

        x_sb = sb(st, "x_sb", [128, NT * D], F32)
        ident_bf = sb(st, "ident_bf", [128, 128], BF16)
        ident_f = sb(st, "ident_f", [128, 128], F32)
        stats = sb(st, "stats", [128, 2 * NT], F32)
        bcol_sb = sb(st, "bcol_sb", [128, NDT], F32)
        blam_sb = sb(st, "blam_sb", [128, NH], F32)
        mask_sb = sb(st, "mask_sb", [128, 8 * 128], BF16)
        sel_sb = sb(st, "sel_sb", [128, 4 * 32], F32)
        zero128 = sb(st, "zero128", [128, 128], F32)
        eps_sb = sb(st, "eps_sb", [128, 1], F32)

        def xt(t):
            return x_sb[:, t * D:(t + 1) * D]

        def xcol(t, dt):
            return x_sb[:, t * D + dt * 128: t * D + dt * 128 + 128]

        def phase_barrier():
            c = dict(cx.n)
            for en in ["tensor", "vector", "scalar", "sync", "gpsimd"]:
                @getattr(block, en)
                def _(e, c=c):
                    for sn in ["sv", "ss", "stt", "sd", "sg", "scc"]:
                        if c[sn] > 0:
                            e.wait_ge(cx.h[sn], c[sn])

        @block.sync
        def _(sync):
            for t in range(NT):
                cx.inc(sync.dma_start(out=xt(t), in_=x8[t]), "sd", 16)
            cx.inc(sync.dma_start(out=ident_bf[:], in_=id_bf), "sd", 16)
            cx.inc(sync.dma_start(out=ident_f[:], in_=id_f32), "sd", 16)
            cx.inc(sync.dma_start(out=bcol_sb[:], in_=b_col), "sd", 16)
            cx.inc(sync.dma_start(out=blam_sb[:], in_=blam_rep), "sd", 16)
            cx.inc(sync.dma_start(out=mask_sb[:], in_=mask8), "sd", 16)
            cx.inc(sync.dma_start(out=sel_sb[:], in_=sel4), "sd", 16)
        n_consts = cx.n["sd"]

        @block.vector
        def _(v):
            cx.wait(v, "sd")
            v.memset(zero128[:], 0.0)
            cx.inc(v.memset(eps_sb[:], 1e-5), "sv")

        def prep(tag, src, n_rows, w, cci, ccm, cco, quant):
            ngr = n_rows // 128
            n_cs = w // 128
            nh = 2 if ngr * w > 8192 else 1
            gh = ngr // nh
            with ExitStack() as s1:
                FREE = gh * w
                qf = sb(s1, tag + "qf", [128, FREE], F32)
                qb = sb(s1, tag + "qb", [128, FREE], BF16)
                stg = sb(s1, tag + "st", [128, ngr * w], BF16)
                if quant:
                    q8 = sb(s1, tag + "q8", [128, FREE], FP8)
                    mx = sb(s1, tag + "mx", [128, FREE // 32], F32)
                    rc = sb(s1, tag + "rc", [128, FREE // 32], F32)
                pq0 = ps(s1, tag + "p0", [128, 128], BF16)
                pq1 = ps(s1, tag + "p1", [128, 128], BF16)
                pipe = TrPipe(cx, pq0, pq1, ident_bf, "ss")

                pe0 = dict(cx.n)
                for hh in range(nh):
                    g0 = hh * gh
                    ch0 = dict(cx.n)

                    @block.sync
                    def _(sync, g0=g0, ch0=ch0, pe0=pe0):
                        if g0:
                            sync.wait_ge(cx.h["sv"], ch0["sv"])
                        else:
                            sync.wait_ge(cx.h["sg"], pe0["sg"])
                            sync.wait_ge(cx.h["ss"], pe0["ss"])
                        cx.inc(sync.dma_start(out=qf[:].rearrange("p (g w) -> p g w", w=w),
                                              in_=src[g0 * 128:(g0 + gh) * 128, :]
                                              .rearrange("(g p) w -> p g w", p=128)), "sd", 16)
                    c0 = dict(cx.n)

                    @block.vector
                    def _(v, c0=c0):
                        v.wait_ge(cx.h["sd"], c0["sd"])
                        v.wait_ge(cx.h["ss"], c0["ss"])
                        if quant:
                            a3 = qf[:].rearrange("p (n k) -> p n k", k=32)
                            cx.inc(v.tensor_reduce(out=mx[:], in_=a3, axis=AX.X, op=ALU.max,
                                                   apply_absolute_value=True), "sv")
                            v.wait_ge(cx.h["sv"], cx.n["sv"])
                            cx.inc(v.tensor_scalar(out=mx[:], in0=mx[:], scalar1=QEPS, scalar2=QSC,
                                                   op0=ALU.max, op1=ALU.mult), "sv")
                            v.wait_ge(cx.h["sv"], cx.n["sv"])
                            cx.inc(v.reciprocal(out=rc[:], in_=mx[:]), "sv")
                            v.wait_ge(cx.h["sv"], cx.n["sv"])
                            v.tensor_tensor(out=q8[:].rearrange("p (n k) -> p n k", k=32), in0=a3,
                                            in1=rc[:].unsqueeze(2).broadcast_to([128, FREE // 32, 32]),
                                            op=ALU.mult)
                            cx.inc(v.tensor_tensor(out=qb[:].rearrange("p (n k) -> p n k", k=32),
                                                   in0=q8[:].rearrange("p (n k) -> p n k", k=32),
                                                   in1=mx[:].unsqueeze(2).broadcast_to([128, FREE // 32, 32]),
                                                   op=ALU.mult), "sv")
                        else:
                            cx.inc(v.tensor_copy(out=qb[:], in_=qf[:]), "sv")
                    c1 = dict(cx.n)

                    in_aps = [qb[:, g * w + cs * 128: g * w + cs * 128 + 128]
                              for g in range(gh) for cs in range(n_cs)]
                    outs = [(cs, g0 + g) for g in range(gh) for cs in range(n_cs)]

                    def ev(eng, i, psum_ap, outs=outs):
                        cs, g = outs[i]
                        return eng.copy(out=stg[:, cs * (ngr * 128) + g * 128:
                                                cs * (ngr * 128) + g * 128 + 128], in_=psum_ap)
                    pipe.group(in_aps, ev, "scalar", pre_waits=[("sv", c1["sv"])])
                c2 = dict(cx.n)

                @block.gpsimd
                def _(g):
                    cx.wait(g, "ss", c2["ss"])
                    cx.inc(g.dma_start(out=cci.rearrange("(cs p) r -> p cs r", p=128),
                                       in_=stg[:].rearrange("p (cs r) -> p cs r", cs=n_cs)), "sg", 16)
                    cx.wait(g, "sg")
                    cx.inc(g.collective_compute("AllGather", ALU.bypass, replica_groups=GRP4,
                                                ins=[cci], outs=[ccm]), "scc")
                    cx.wait(g, "scc")
                    cx.inc(g.collective_compute("AllGather", ALU.bypass, replica_groups=GRP2,
                                                ins=[ccm], outs=[cco]), "scc")

        prep("pA", wa_sl, D, 256, ccA_i, ccA_m, aWT, quant=False)
        n_scc_awt = cx.n["scc"]
        if phases >= 2:
            phase_barrier()
            prep("pG", gate_sl, ISZ, 256, ccG_i, ccG_m, gateT, quant=True)
            phase_barrier()
            prep("pU", up_sl, ISZ, 256, ccU_i, ccU_m, upT, quant=True)
            phase_barrier()
            prep("pD", down_sl, D, 1024, ccD_i, ccD_m, downT, quant=True)
            phase_barrier()
        n_scc_ffn = cx.n["scc"]

        pre_ph1 = dict(cx.n)
        for en in ["tensor", "vector", "scalar"]:
            @getattr(block, en)
            def _(e):
                e.wait_ge(cx.h["sg"], pre_ph1["sg"])
                e.wait_ge(cx.h["ss"], pre_ph1["ss"])
                e.wait_ge(cx.h["sv"], pre_ph1["sv"])

        def emit_ln_stats(tmpA, tmpB):
            for t in range(NT):
                tmp = tmpA if t % 2 == 0 else tmpB
                ss_before = cx.n["ss"]

                @block.vector
                def _(v, t=t, tmp=tmp, w=ss_before):
                    if t >= 2:
                        v.wait_ge(cx.h["ss"], w)
                    cx.inc(v.tensor_reduce(out=stats[:, 2 * t:2 * t + 1], in_=xt(t), axis=AX.X, op=ALU.add), "sv")
                    v.wait_ge(cx.h["sv"], cx.n["sv"])
                    cx.inc(v.tensor_scalar(out=stats[:, 2 * t:2 * t + 1], in0=stats[:, 2 * t:2 * t + 1],
                                           scalar1=1.0 / D, scalar2=None, op0=ALU.mult), "sv")
                    v.wait_ge(cx.h["sv"], cx.n["sv"])
                    cx.inc(v.tensor_scalar(out=tmp[:], in0=xt(t), scalar1=stats[:, 2 * t:2 * t + 1],
                                           scalar2=None, op0=ALU.subtract), "sv")
                cv = cx.n["sv"]

                @block.scalar
                def _(sc, t=t, tmp=tmp, cv=cv):
                    sc.wait_ge(cx.h["sv"], cv)
                    cx.inc(sc.activation(out=tmp[:], in_=tmp[:], func=ACT.Square,
                                         accum_out=stats[:, 2 * t + 1:2 * t + 2]), "ss")
                    sc.wait_ge(cx.h["ss"], cx.n["ss"])
                    cx.inc(sc.activation(out=stats[:, 2 * t + 1:2 * t + 2],
                                         in_=stats[:, 2 * t + 1:2 * t + 2],
                                         func=ACT.Sqrt, bias=eps_sb[:], scale=1.0 / D), "ss")
                cs2 = cx.n["ss"]

                @block.vector
                def _(v, t=t, cs2=cs2):
                    v.wait_ge(cx.h["ss"], cs2)
                    cx.inc(v.reciprocal(out=stats[:, 2 * t + 1:2 * t + 2],
                                        in_=stats[:, 2 * t + 1:2 * t + 2]), "sv")

        def ln_ts(v, out_ap, t, dt=None):
            src = xt(t) if dt is None else xcol(t, dt)
            return v.tensor_scalar(out=out_ap, in0=src, scalar1=stats[:, 2 * t:2 * t + 1],
                                   scalar2=stats[:, 2 * t + 1:2 * t + 2],
                                   op0=ALU.subtract, op1=ALU.mult)

        def emit_xlnT(xlnT, tmpA, tmpB, pipe_f, out_dtype_evict="scalar"):
            pre_ss = cx.n["ss"]
            pre_sv = cx.n["sv"]
            for t in range(NT):
                tmp = tmpA if t % 2 == 0 else tmpB
                tt_before = cx.n["stt"]

                @block.vector
                def _(v, t=t, tmp=tmp, w=tt_before, pre_ss=pre_ss, pre_sv=pre_sv):
                    if t < 2:
                        v.wait_ge(cx.h["ss"], pre_ss)
                        v.wait_ge(cx.h["sv"], pre_sv)
                    else:
                        v.wait_ge(cx.h["stt"], w)
                    cx.inc(ln_ts(v, tmp[:], t), "sv")
                cv = cx.n["sv"]

                def ev(eng, i, psum_ap, t=t):
                    return eng.copy(out=xlnT[:, i * 1024 + t * 128: i * 1024 + t * 128 + 128],
                                    in_=psum_ap)
                pipe_f.group([tmp[:, dt * 128:(dt + 1) * 128] for dt in range(NDT)],
                             ev, out_dtype_evict, pre_waits=[("sv", cv)])

        if phases >= 1:
            with ExitStack() as s1:
                xlnT = sb(s1, "a_xlnT", [128, NDT * 1024], BF16)
                P_g = sb(s1, "a_Pg", [128, NDT * 1024], BF16)
                ptmp = sb(s1, "a_ptmp", [128, 1024], F32)
                tmpA = sb(s1, "a_tmpA", [128, D], F32)
                tmpB = sb(s1, "a_tmpB", [128, D], F32)
                awt0 = sb(s1, "a_awt0", [128, NDT * 128], BF16)
                awt1 = sb(s1, "a_awt1", [128, NDT * 128], BF16)
                lam_t = sb(s1, "a_lam", [128, 1024], F32)
                u_dt = sb(s1, "a_u", [128, 1024], F32)
                ys_dt = sb(s1, "a_ys", [128, 1024], F32)
                bsc = sb(s1, "a_bsc", [128, 1024], F32)
                stg_sc = sb(s1, "a_stg", [128, 256], F32)
                scm = sb(s1, "a_scm", [128, 2048], F32)
                cand = sb(s1, "a_cand", [128, 16 * 32], F32)
                s_run = sb(s1, "a_srun", [128, 16], F32)
                s0_all = sb(s1, "a_s0", [128, NT * NDT], F32)
                sel_tmp = sb(s1, "a_selt", [128, 512], F32)
                pta = ps(s1, "a_pta", [128, 128], F32)
                ptb = ps(s1, "a_ptb", [128, 128], F32)
                ptc = ps(s1, "a_ptc", [128, 128], F32)
                ptd = ps(s1, "a_ptd", [128, 128], F32)
                pml0 = ps(s1, "a_pml0", [128, 512], F32)
                pml1 = ps(s1, "a_pml1", [128, 512], F32)

                pipe_f = TrPipe(cx, pta, ptb, ident_f, "sv")
                pipe_b = TrPipe(cx, ptc, ptd, ident_f, "ss")

                emit_ln_stats(tmpA, tmpB)
                emit_xlnT(xlnT, tmpA, tmpB, pipe_b)
                c_ln = dict(cx.n)

                mm_done = []
                sig_done = []
                scan_done = []
                for dt in range(NDT):
                    awt = awt0 if dt % 2 == 0 else awt1

                    @block.sync
                    def _(sync, dt=dt, awt=awt):
                        if dt == 0:
                            sync.wait_ge(cx.h["scc"], n_scc_awt)
                        if dt >= 2:
                            sync.wait_ge(cx.h["stt"], mm_done[dt - 2])
                        cx.inc(sync.dma_start(
                            out=awt[:].rearrange("p (ds e) -> p ds e", e=128),
                            in_=aWT[:, dt * 128:dt * 128 + 128].rearrange("(ds p) e -> p ds e", p=128)),
                            "sd", 16)
                    c_l = cx.n["sd"]
                    if dt == 0:
                        @block.gpsimd
                        def _(g, awt=awt, c_l=c_l):
                            g.wait_ge(cx.h["sd"], c_l)
                            cx.inc(g.dma_start(out=dbg[2], in_=awt[:, 0:1024]), "sg", 16)
                            cx.inc(g.dma_start(out=dbg[3, :, 0:1024], in_=awt[:, 1024:2048]), "sg", 16)

                    @block.tensor
                    def _(te, dt=dt, awt=awt, c_l=c_l):
                        te.wait_ge(cx.h["sd"], c_l)
                        if dt == 0:
                            te.wait_ge(cx.h["ss"], c_ln["ss"])
                        else:
                            te.wait_ge(cx.h["ss"], sig_done[dt - 1])
                        for ds in range(NDT):
                            for ch in range(2):
                                mm = te.matmul(out=(pml0 if ch == 0 else pml1)[:],
                                               lhsT=awt[:, ds * 128:(ds + 1) * 128],
                                               rhs=xlnT[:, ds * 1024 + ch * 512: ds * 1024 + (ch + 1) * 512],
                                               start=(ds == 0), stop=(ds == NDT - 1))
                                if ds == NDT - 1:
                                    cx.inc(mm, "stt")
                    mm_done.append(cx.n["stt"])

                    @block.scalar
                    def _(sc, dt=dt):
                        sc.wait_ge(cx.h["stt"], mm_done[dt])
                        if dt >= 1:
                            sc.wait_ge(cx.h["sv"], scan_done[dt - 1])
                        for ch in range(2):
                            cx.inc(sc.activation(out=lam_t[:, ch * 512:(ch + 1) * 512],
                                                 in_=(pml0 if ch == 0 else pml1)[:],
                                                 func=ACT.Sigmoid, bias=bcol_sb[:, dt:dt + 1]), "ss")
                    sig_done.append(cx.n["ss"])

                    tt_now = cx.n["stt"]

                    @block.vector
                    def _(v, dt=dt, w=tt_now):
                        v.wait_ge(cx.h["stt"], w)
                        for t in range(NT):
                            cx.inc(ln_ts(v, tmpA[:, t * 128:(t + 1) * 128], t, dt=dt), "sv")
                    cu = cx.n["sv"]

                    def ev_u(eng, i, psum_ap):
                        return eng.tensor_copy(out=u_dt[:, i * 128:(i + 1) * 128], in_=psum_ap)
                    pipe_f.group([tmpA[:, t * 128:(t + 1) * 128] for t in range(NT)],
                                 ev_u, "vector", pre_waits=[("sv", cu)])
                    tt_ys = cx.n["stt"]

                    @block.vector
                    def _(v, dt=dt, tt_ys=tt_ys):
                        v.wait_ge(cx.h["stt"], tt_ys)
                        v.wait_ge(cx.h["ss"], sig_done[dt])
                        v.wait_ge(cx.h["sv"], cx.n["sv"])
                        cx.inc(v.tensor_scalar(out=bsc[:], in0=lam_t[:], scalar1=-1.0, scalar2=1.0,
                                               op0=ALU.mult, op1=ALU.add), "sv")
                        v.wait_ge(cx.h["sv"], cx.n["sv"])
                        cx.inc(v.tensor_tensor(out=bsc[:], in0=bsc[:], in1=u_dt[:], op=ALU.mult), "sv")
                        v.wait_ge(cx.h["sv"], cx.n["sv"])
                        for blk in range(NT):
                            sl = slice(blk * 128, blk * 128 + 128)
                            v.tensor_tensor_scan(out=ys_dt[:, sl], data0=lam_t[:, sl],
                                                 data1=bsc[:, sl], initial=0.0,
                                                 op0=ALU.mult, op1=ALU.add)
                            sc_i = v.tensor_tensor_scan(
                                out=ptmp[:, sl],
                                data0=lam_t[:, sl], data1=zero128[:],
                                initial=1.0, op0=ALU.mult, op1=ALU.add)
                            if blk == NT - 1:
                                cx.inc(sc_i, "sv")
                        v.wait_ge(cx.h["sv"], cx.n["sv"])
                        v.tensor_copy(out=P_g[:, dt * 1024:(dt + 1) * 1024], in_=ptmp[:])
                        for blk in range(NT):
                            v.tensor_copy(out=stg_sc[:, blk * 32 + dt: blk * 32 + dt + 1],
                                          in_=ptmp[:, blk * 128 + 127: blk * 128 + 128])
                            cx.inc(v.tensor_copy(out=stg_sc[:, blk * 32 + 16 + dt: blk * 32 + 16 + dt + 1],
                                                 in_=ys_dt[:, blk * 128 + 127: blk * 128 + 128]), "sv")
                    scan_done.append(cx.n["sv"])
                    if dt == 0:
                        @block.sync
                        def _(sync):
                            sync.wait_ge(cx.h["sv"], scan_done[0])
                            cx.inc(sync.dma_start(out=dbg[0], in_=lam_t[:]), "sd", 16)
                            cx.inc(sync.dma_start(out=dbg[1], in_=u_dt[:]), "sd", 16)
                            cx.inc(sync.dma_start(out=dbg[2], in_=ys_dt[:]), "sd", 16)
                            cx.inc(sync.dma_start(out=dbg[3, :, 0:16], in_=stats[:]), "sd", 16)

                    def ev_y(eng, i, psum_ap, dt=dt):
                        return eng.tensor_tensor(out=xcol(i, dt), in0=psum_ap,
                                                 in1=xcol(i, dt), op=ALU.add)
                    pipe_f.group([ys_dt[:, t * 128:(t + 1) * 128] for t in range(NT)],
                                 ev_y, "vector", pre_waits=[("sv", scan_done[dt])])

                c_scan = dict(cx.n)

                @block.gpsimd
                def _(g):
                    cx.wait(g, "sv", c_scan["sv"])
                    cx.inc(g.dma_start(out=ccS_i, in_=stg_sc[:]), "sg", 16)
                    cx.wait(g, "sg")
                    cx.inc(g.collective_compute("AllGather", ALU.bypass, replica_groups=GRP4,
                                                ins=[ccS_i], outs=[ccS_m]), "scc")
                    cx.wait(g, "scc")
                    cx.inc(g.collective_compute("AllGather", ALU.bypass, replica_groups=GRP2,
                                                ins=[ccS_m], outs=[ccS_o]), "scc")
                    cx.wait(g, "scc")
                    for c8 in range(8):
                        cx.inc(g.dma_start(out=scm[:, c8 * 256:(c8 + 1) * 256],
                                           in_=ccS_o[c8 * 128:(c8 + 1) * 128, :]), "sg", 16)
                c_sx = dict(cx.n)

                @block.vector
                def _(v):
                    cx.wait(v, "sg", c_sx["sg"])
                    cnd = cand[:].rearrange("p (d m) -> p d m", m=32)
                    for b in range(NB):
                        cx.inc(v.memset(s_run[:], 0.0), "sv")
                        v.wait_ge(cx.h["sv"], cx.n["sv"])
                        for m in range(32):
                            c8, sblk = m % 8, m // 8
                            blk = b * 4 + sblk
                            base = c8 * 256 + blk * 32
                            v.tensor_copy(out=cnd[:, :, m:m + 1],
                                          in_=s_run[:].unsqueeze(2))
                            cx.inc(v.tensor_tensor(out=s_run[:], in0=s_run[:],
                                                   in1=scm[:, base:base + 16], op=ALU.mult), "sv")
                            v.wait_ge(cx.h["sv"], cx.n["sv"])
                            cx.inc(v.tensor_tensor(out=s_run[:], in0=s_run[:],
                                                   in1=scm[:, base + 16:base + 32], op=ALU.add), "sv")
                            v.wait_ge(cx.h["sv"], cx.n["sv"])
                        for s in range(4):
                            t = b * 4 + s
                            cx.inc(v.tensor_tensor(out=sel_tmp[:].rearrange("p (d m) -> p d m", m=32),
                                                   in0=cnd,
                                                   in1=sel_sb[:, s * 32:(s + 1) * 32].unsqueeze(1)
                                                   .broadcast_to([128, 16, 32]), op=ALU.mult), "sv")
                            v.wait_ge(cx.h["sv"], cx.n["sv"])
                            cx.inc(v.tensor_reduce(out=s0_all[:, t * NDT:(t + 1) * NDT],
                                                   in_=sel_tmp[:].rearrange("p (d m) -> p d m", m=32),
                                                   axis=AX.X, op=ALU.add), "sv")
                            v.wait_ge(cx.h["sv"], cx.n["sv"])
                c_s0 = dict(cx.n)

                @block.vector
                def _(v):
                    v.wait_ge(cx.h["sv"], c_s0["sv"])
                    cx.inc(v.tensor_copy(out=ys_dt[:, 0:1024], in_=awt0[:, 0:1024]), "sv")
                c_awdump = dict(cx.n)

                @block.sync
                def _(sync):
                    sync.wait_ge(cx.h["sv"], c_awdump["sv"])
                    cx.inc(sync.dma_start(out=dbg[1], in_=ys_dt[:, 0:1024]), "sd", 16)

                @block.sync
                def _(sync):
                    sync.wait_ge(cx.h["sv"], c_s0["sv"])
                    cx.inc(sync.dma_start(out=dbg[3, :, 16:16 + 128], in_=s0_all[:]), "sd", 16)
                    cx.inc(sync.dma_start(out=dbg[3, :, 256:256 + 512], in_=cand[:]), "sd", 16)
                    cx.inc(sync.dma_start(out=dbg[1], in_=scm[:, 0:1024]), "sd", 16)
                    cx.inc(sync.dma_start(out=dbg[2], in_=scm[:, 1024:2048]), "sd", 16)

                for dt in range(NDT):
                    @block.vector
                    def _(v, dt=dt):
                        if dt == 0:
                            cx.wait(v, "sv", c_s0["sv"])
                        for t in range(NT):
                            cx.inc(v.tensor_scalar(
                                out=ys_dt[:, t * 128:(t + 1) * 128],
                                in0=P_g[:, dt * 1024 + t * 128: dt * 1024 + (t + 1) * 128],
                                scalar1=s0_all[:, t * NDT + dt: t * NDT + dt + 1],
                                scalar2=None, op0=ALU.mult), "sv")
                    cf = cx.n["sv"]

                    def ev_f(eng, i, psum_ap, dt=dt):
                        return eng.tensor_tensor(out=xcol(i, dt), in0=psum_ap,
                                                 in1=xcol(i, dt), op=ALU.add)
                    pipe_f.group([ys_dt[:, t * 128:(t + 1) * 128] for t in range(NT)],
                                 ev_f, "vector", pre_waits=[("sv", cf)])


        if phases >= 1:
            phase_barrier()

        def vbar(v):
            v.wait_ge(cx.h["sv"], cx.n["sv"])

        def emit_ffn(idx):
            with ExitStack() as s2:
                xq_T = sb(s2, f"f{idx}_xqT", [128, NDT * 1024], BF16)
                hqT = sb(s2, f"f{idx}_hqT", [128, NDT * 1024], BF16)

                with ExitStack() as s3:
                    tmpA = sb(s3, f"f{idx}_tA", [128, D], F32)
                    tmpB = sb(s3, f"f{idx}_tB", [128, D], F32)
                    q8x = sb(s3, f"f{idx}_q8", [128, D], FP8)
                    mxs = sb(s3, f"f{idx}_mx", [128, 64], F32)
                    rcs = sb(s3, f"f{idx}_rc", [128, 64], F32)
                    xqb = sb(s3, f"f{idx}_xqb", [128, D], BF16)
                    ptA = ps(s3, f"f{idx}_ptA", [128, 128], BF16)
                    ptB = ps(s3, f"f{idx}_ptB", [128, 128], BF16)
                    pipe2 = TrPipe(cx, ptA, ptB, ident_bf, "ss")
                    emit_ln_stats(tmpA, tmpB)
                    ln_done = dict(cx.n)
                    for t in range(NT):
                        w0 = dict(cx.n)

                        @block.vector
                        def _(v, t=t, w0=w0):
                            if t == 0:
                                v.wait_ge(cx.h["ss"], ln_done["ss"])
                                v.wait_ge(cx.h["sv"], ln_done["sv"])
                            else:
                                v.wait_ge(cx.h["ss"], w0["ss"])
                            cx.inc(ln_ts(v, tmpA[:], t), "sv")
                            vbar(v)
                            a3 = tmpA[:].rearrange("p (n k) -> p n k", k=32)
                            cx.inc(v.tensor_reduce(out=mxs[:], in_=a3, axis=AX.X, op=ALU.max,
                                                   apply_absolute_value=True), "sv")
                            vbar(v)
                            cx.inc(v.tensor_scalar(out=mxs[:], in0=mxs[:], scalar1=QEPS, scalar2=QSC,
                                                   op0=ALU.max, op1=ALU.mult), "sv")
                            vbar(v)
                            cx.inc(v.reciprocal(out=rcs[:], in_=mxs[:]), "sv")
                            vbar(v)
                            cx.inc(v.tensor_tensor(out=q8x[:].rearrange("p (n k) -> p n k", k=32),
                                                   in0=a3,
                                                   in1=rcs[:].unsqueeze(2).broadcast_to([128, 64, 32]),
                                                   op=ALU.mult), "sv")
                            vbar(v)
                            cx.inc(v.tensor_tensor(out=xqb[:].rearrange("p (n k) -> p n k", k=32),
                                                   in0=q8x[:].rearrange("p (n k) -> p n k", k=32),
                                                   in1=mxs[:].unsqueeze(2).broadcast_to([128, 64, 32]),
                                                   op=ALU.mult), "sv")
                        cq = cx.n["sv"]

                        def evq(eng, i, psum_ap, t=t):
                            return eng.copy(out=xq_T[:, i * 1024 + t * 128: i * 1024 + t * 128 + 128],
                                            in_=psum_ap)
                        pipe2.group([xqb[:, dt * 128:(dt + 1) * 128] for dt in range(NDT)],
                                    evq, "scalar", pre_waits=[("sv", cq)])
                phase_barrier()
                xq_done = dict(cx.n)
                if idx == 0:
                    @block.gpsimd
                    def _(g):
                        g.wait_ge(cx.h["ss"], xq_done["ss"])
                        cx.inc(g.dma_start(out=dbg[0], in_=xq_T[:, 0:1024]), "sg", 16)
                        cx.inc(g.dma_start(out=dbg[1, :, 0:64], in_=mxs[:]), "sg", 16)
                        cx.inc(g.dma_start(out=dbg[1, :, 64:128], in_=rcs[:]), "sg", 16)
                        cx.inc(g.dma_start(out=dbg[1, :, 256:256 + 512], in_=xqb[:, 0:512]), "sg", 16)
                        cx.inc(g.dma_start(out=dbg[1, :, 128:128 + 16], in_=stats[:]), "sg", 16)
                        cx.inc(g.dma_start(out=dbg[3, :, 512:1024], in_=x_sb[:, 0:512]), "sg", 16)

                for Q in range(4):
                    with ExitStack() as s4:
                        wg = [sb(s4, f"f{idx}q{Q}_wg{i}", [128, NDT * 512], BF16) for i in range(2)]
                        wu = [sb(s4, f"f{idx}q{Q}_wu{i}", [128, NDT * 512], BF16) for i in range(2)]
                        hbuf = sb(s4, f"f{idx}q{Q}_h", [128, 512], F32)
                        sgl = sb(s4, f"f{idx}q{Q}_sg", [128, 512], F32)
                        q8h = sb(s4, f"f{idx}q{Q}_q8h", [128, 512], FP8)
                        mxh = sb(s4, f"f{idx}q{Q}_mxh", [128, 16], F32)
                        rch = sb(s4, f"f{idx}q{Q}_rch", [128, 16], F32)
                        hqb = sb(s4, f"f{idx}q{Q}_hqb", [128, 512], BF16)
                        pg = [ps(s4, f"f{idx}q{Q}_pg{i}", [128, 512], F32) for i in range(2)]
                        pu = [ps(s4, f"f{idx}q{Q}_pu{i}", [128, 512], F32) for i in range(2)]
                        ptC = ps(s4, f"f{idx}q{Q}_ptC", [128, 128], BF16)
                        ptD = ps(s4, f"f{idx}q{Q}_ptD", [128, 128], BF16)
                        pipe3 = TrPipe(cx, ptC, ptD, ident_bf, "ss")
                        use_done = []
                        for ci in range(4):
                            c = Q * 4 + ci
                            slot = ci % 2

                            @block.sync
                            def _(sync, c=c, ci=ci, slot=slot):
                                if c == 0:
                                    sync.wait_ge(cx.h["scc"], n_scc_ffn)
                                if ci >= 2:
                                    sync.wait_ge(cx.h["stt"], use_done[ci - 2])
                                for W_, buf in ((gateT, wg[slot]), (upT, wu[slot])):
                                    cx.inc(sync.dma_start(
                                        out=buf[:].rearrange("p (g w) -> p g w", w=512),
                                        in_=W_[:, c * 512:(c + 1) * 512].rearrange("(g p) w -> p g w", p=128)),
                                        "sd", 16)
                            ldn = cx.n["sd"]

                            for tokt in range(NT):
                                pgb, pub = pg[tokt % 2], pu[tokt % 2]
                                ev_prev = cx.n["sv"]

                                @block.tensor
                                def _(te, ci=ci, slot=slot, tokt=tokt, ldn=ldn, ev_prev=ev_prev):
                                    if tokt == 0:
                                        te.wait_ge(cx.h["sd"], ldn)
                                        if ci == 0 and Q == 0:
                                            te.wait_ge(cx.h["ss"], xq_done["ss"])
                                    if tokt >= 2:
                                        te.wait_ge(cx.h["sv"], ev_prev)
                                    for ds in range(NDT):
                                        lhs = xq_T[:, ds * 1024 + tokt * 128: ds * 1024 + tokt * 128 + 128]
                                        te.matmul(out=pgb[:], lhsT=lhs,
                                                  rhs=wg[slot][:, ds * 512:(ds + 1) * 512],
                                                  start=(ds == 0), stop=(ds == NDT - 1))
                                        mmu = te.matmul(out=pub[:], lhsT=lhs,
                                                        rhs=wu[slot][:, ds * 512:(ds + 1) * 512],
                                                        start=(ds == 0), stop=(ds == NDT - 1))
                                        if ds == NDT - 1:
                                            cx.inc(mmu, "stt")
                                mm_n = cx.n["stt"]

                                @block.scalar
                                def _(sc, mm_n=mm_n, tokt=tokt, pgb=pgb):
                                    sc.wait_ge(cx.h["stt"], mm_n)
                                    sc.wait_ge(cx.h["sv"], cx.n["sv"])
                                    cx.inc(sc.activation(out=sgl[:], in_=pgb[:], func=ACT.Silu), "ss")
                                sg_n = cx.n["ss"]

                                @block.vector
                                def _(v, sg_n=sg_n, pub=pub):
                                    v.wait_ge(cx.h["ss"], sg_n)
                                    cx.inc(v.tensor_tensor(out=hbuf[:], in0=pub[:], in1=sgl[:], op=ALU.mult), "sv")
                                    vbar(v)
                                    a3 = hbuf[:].rearrange("p (n k) -> p n k", k=32)
                                    cx.inc(v.tensor_reduce(out=mxh[:], in_=a3, axis=AX.X, op=ALU.max,
                                                           apply_absolute_value=True), "sv")
                                    vbar(v)
                                    cx.inc(v.tensor_scalar(out=mxh[:], in0=mxh[:], scalar1=QEPS, scalar2=QSC,
                                                           op0=ALU.max, op1=ALU.mult), "sv")
                                    vbar(v)
                                    cx.inc(v.reciprocal(out=rch[:], in_=mxh[:]), "sv")
                                    vbar(v)
                                    cx.inc(v.tensor_tensor(out=q8h[:].rearrange("p (n k) -> p n k", k=32),
                                                           in0=a3,
                                                           in1=rch[:].unsqueeze(2).broadcast_to([128, 16, 32]),
                                                           op=ALU.mult), "sv")
                                    vbar(v)
                                    cx.inc(v.tensor_tensor(out=hqb[:].rearrange("p (n k) -> p n k", k=32),
                                                           in0=q8h[:].rearrange("p (n k) -> p n k", k=32),
                                                           in1=mxh[:].unsqueeze(2).broadcast_to([128, 16, 32]),
                                                           op=ALU.mult), "sv")
                                hq_n = cx.n["sv"]
                                if idx == 0 and Q == 0 and ci == 0 and tokt == 0:
                                    @block.gpsimd
                                    def _(g, hq_n=hq_n):
                                        g.wait_ge(cx.h["sv"], hq_n)
                                        cx.inc(g.dma_start(out=dbg[2, :, 0:512], in_=hbuf[:]), "sg", 16)
                                        cx.inc(g.dma_start(out=dbg[2, :, 512:1024], in_=sgl[:]), "sg", 16)
                                        cx.inc(g.dma_start(out=dbg[3, :, 0:512], in_=hqb[:]), "sg", 16)

                                def evh(eng, i, psum_ap, ci=ci, tokt=tokt):
                                    it = ci * 4 + i
                                    return eng.copy(out=hqT[:, it * 1024 + tokt * 128: it * 1024 + tokt * 128 + 128],
                                                    in_=psum_ap)
                                pipe3.group([hqb[:, j * 128:(j + 1) * 128] for j in range(4)],
                                            evh, "scalar", pre_waits=[("sv", hq_n)])
                            use_done.append(cx.n["stt"])
                    phase_barrier()
                    gu_done = dict(cx.n)

                    with ExitStack() as s5:
                        ptmp_dbg = sb(s5, f"f{idx}q{Q}_pdbg", [128, 2048], F32)
                        dwn = [sb(s5, f"f{idx}q{Q}_dw{i}", [128, 512], BF16) for i in range(2)]
                        pd = [ps(s5, f"f{idx}q{Q}_pd{i}", [128, 512], F32) for i in range(8)]
                        dmm = []
                        for Hc in range(4):
                            for it in range(16):
                                k_ = Hc * 16 + it
                                slot = k_ % 2

                                @block.sync
                                def _(sync, Q=Q, Hc=Hc, it=it, k_=k_, slot=slot):
                                    if Hc == 0 and it == 0:
                                        sync.wait_ge(cx.h["ss"], gu_done["ss"])
                                    if k_ >= 2:
                                        sync.wait_ge(cx.h["stt"], dmm[k_ - 2])
                                    cx.inc(sync.dma_start(
                                        out=dwn[slot][:],
                                        in_=downT[(Q * 16 + it) * 128:(Q * 16 + it + 1) * 128,
                                                  Hc * 512:(Hc + 1) * 512]), "sd", 16)
                                ldn2 = cx.n["sd"]

                                @block.tensor
                                def _(te, it=it, slot=slot, ldn2=ldn2, Hc=Hc):
                                    te.wait_ge(cx.h["sd"], ldn2)
                                    if it == 0:
                                        te.wait_ge(cx.h["sv"], cx.n["sv"])
                                    for tokt in range(NT):
                                        mm = te.matmul(out=pd[tokt][:],
                                                       lhsT=hqT[:, it * 1024 + tokt * 128: it * 1024 + tokt * 128 + 128],
                                                       rhs=dwn[slot][:],
                                                       start=(it == 0), stop=(it == NDT - 1))
                                        if tokt == NT - 1:
                                            cx.inc(mm, "stt")
                                dmm.append(cx.n["stt"])

                            dn = cx.n["stt"]

                            @block.vector
                            def _(v, Hc=Hc, dn=dn):
                                v.wait_ge(cx.h["stt"], dn)
                                for tokt in range(NT):
                                    cx.inc(v.tensor_tensor(
                                        out=x_sb[:, tokt * D + Hc * 512: tokt * D + (Hc + 1) * 512],
                                        in0=pd[tokt][:],
                                        in1=x_sb[:, tokt * D + Hc * 512: tokt * D + (Hc + 1) * 512],
                                        op=ALU.add), "sv")
                    phase_barrier()

        def emit_ffn_outer(idx):
            emit_ffn(idx)
            phase_barrier()

        if phases >= 2 and not SKIP_FFN:
            emit_ffn_outer(0)

        if phases >= 3:
            with ExitStack() as s6:
                lam_sb = sb(s6, "b_lam", [128, NT * NH], F32)
                with ExitStack() as s6b:
                    xlnT = sb(s6b, "b_xlnT", [128, NDT * 1024], BF16)
                    c_kvT = sb(s6b, "b_ckv", [128, 4 * 1024], BF16)
                    c_qT = sb(s6b, "b_cq", [128, 8 * 1024], BF16)
    
                    with ExitStack() as s7:
                        tmpA = sb(s7, "b_tA", [128, D], F32)
                        tmpB = sb(s7, "b_tB", [128, D], F32)
                        ptA = ps(s7, "b_ptA", [128, 128], F32)
                        ptB = ps(s7, "b_ptB", [128, 128], F32)
                        pipe4 = TrPipe(cx, ptA, ptB, ident_f, "ss")
                        emit_ln_stats(tmpA, tmpB)
                        emit_xlnT(xlnT, tmpA, tmpB, pipe4)
                    phase_barrier()
                    ln3 = dict(cx.n)
    
                    with ExitStack() as s8:
                        cos_sb = sb(s8, "b_cos", [128, 1024], F32)
                        sinp_sb = sb(s8, "b_sin", [128, 1024], F32)
                        wstg = sb(s8, "b_wstg", [128, 4096], F32)
                        wbf = sb(s8, "b_wbf", [128, 16384], BF16)
    
                        @block.sync
                        def _(sync):
                            cx.inc(sync.dma_start(out=cos_sb[:], in_=cos_t), "sd", 16)
                            cx.inc(sync.dma_start(out=sinp_sb[:], in_=sinp_t), "sd", 16)
                        rope_ld = cx.n["sd"]
                        ktmp = sb(s8, "b_ktmp", [128, 1024], F32)
                        rot = sb(s8, "b_rot", [128, 1024], F32)
                        kro = sb(s8, "b_kro", [128, 1024], BF16)
                        pj = [ps(s8, "b_pj%d" % i, [128, 512], F32) for i in range(8)]
    
                        def cast_w(src_ap, ncols, tag):
                            ng = src_ap.shape[0] // 128
                            per = max(1, 4096 // ncols)
                            c0 = dict(cx.n)
                            for r0 in range(0, ng, per):
                                r1 = min(r0 + per, ng)
                                cw0 = dict(cx.n)
    
                                @block.sync
                                def _(sync, r0=r0, r1=r1, c0=c0, cw0=cw0):
                                    if r0 == 0:
                                        sync.wait_ge(cx.h["sv"], c0["sv"])
                                        sync.wait_ge(cx.h["stt"], c0["stt"])
                                    else:
                                        sync.wait_ge(cx.h["sv"], cw0["sv"])
                                    cx.inc(sync.dma_start(
                                        out=wstg[:, 0:(r1 - r0) * ncols].rearrange("p (g w) -> p g w", w=ncols),
                                        in_=src_ap[r0 * 128: r1 * 128, :].rearrange("(g p) w -> p g w", p=128)),
                                        "sd", 16)
                                c1 = cx.n["sd"]
    
                                @block.vector
                                def _(v, c1=c1, r0=r0, r1=r1):
                                    v.wait_ge(cx.h["sd"], c1)
                                    cx.inc(v.tensor_copy(out=wbf[:, r0 * ncols: r1 * ncols],
                                                         in_=wstg[:, 0:(r1 - r0) * ncols]), "sv")
                            return cx.n["sv"]
    
                        wn = cast_w(wdkv, 512, "dkv")
                        for ct in range(4):
                            @block.tensor
                            def _(te, ct=ct, wn=wn):
                                te.wait_ge(cx.h["sv"], wn)
                                if ct == 0:
                                    te.wait_ge(cx.h["ss"], ln3["ss"])
                                if ct >= 2:
                                    te.wait_ge(cx.h["ss"], cx.n["ss"])
                                for ch in range(2):
                                    for ds in range(NDT):
                                        mm = te.matmul(out=pj[ct % 2 * 2 + ch][:],
                                                       lhsT=wbf[:, ds * 512 + ct * 128: ds * 512 + ct * 128 + 128],
                                                       rhs=xlnT[:, ds * 1024 + ch * 512: ds * 1024 + (ch + 1) * 512],
                                                       start=(ds == 0), stop=(ds == NDT - 1))
                                        if ds == NDT - 1:
                                            cx.inc(mm, "stt")
                            ctn = cx.n["stt"]
    
                            @block.scalar
                            def _(sc, ct=ct, ctn=ctn):
                                sc.wait_ge(cx.h["stt"], ctn)
                                for ch in range(2):
                                    cx.inc(sc.copy(out=c_kvT[:, ct * 1024 + ch * 512: ct * 1024 + (ch + 1) * 512],
                                                   in_=pj[ct % 2 * 2 + ch][:]), "ss")
                        ckv_done = dict(cx.n)
    
                        wn = cast_w(wuk, 2048, "uk")
                        for et in range(16):
                            @block.tensor
                            def _(te, et=et, wn=wn):
                                te.wait_ge(cx.h["sv"], wn)
                                if et == 0:
                                    te.wait_ge(cx.h["ss"], ckv_done["ss"])
                                if et >= 2:
                                    te.wait_ge(cx.h["ss"], cx.n["ss"])
                                for ch in range(2):
                                    for ct in range(4):
                                        mm = te.matmul(out=pj[et % 2 * 2 + ch][:],
                                                       lhsT=wbf[:, ct * 2048 + et * 128: ct * 2048 + et * 128 + 128],
                                                       rhs=c_kvT[:, ct * 1024 + ch * 512: ct * 1024 + (ch + 1) * 512],
                                                       start=(ct == 0), stop=(ct == 3))
                                        if ct == 3:
                                            cx.inc(mm, "stt")
                            ktn = cx.n["stt"]
    
                            @block.scalar
                            def _(sc, et=et, ktn=ktn):
                                sc.wait_ge(cx.h["stt"], ktn)
                                sc.wait_ge(cx.h["sg"], cx.n["sg"])
                                for ch in range(2):
                                    cx.inc(sc.copy(out=ktmp[:, ch * 512:(ch + 1) * 512],
                                                   in_=pj[et % 2 * 2 + ch][:]), "ss")
                            kevn = cx.n["ss"]
    
                            @block.gpsimd
                            def _(g, kevn=kevn):
                                g.wait_ge(cx.h["ss"], kevn)
                                cx.inc(g.dma_start(out=rot[0:64, :], in_=ktmp[64:128, :]), "sg", 16)
                                cx.inc(g.dma_start(out=rot[64:128, :], in_=ktmp[0:64, :]), "sg", 16)
                            rotn = cx.n["sg"]
    
                            @block.vector
                            def _(v, et=et, rotn=rotn, kevn=kevn):
                                if et == 0:
                                    v.wait_ge(cx.h["sd"], rope_ld)
                                v.wait_ge(cx.h["ss"], kevn)
                                v.wait_ge(cx.h["sg"], rotn)
                                v.wait_ge(cx.h["sv"], cx.n["sv"])
                                cx.inc(v.tensor_tensor(out=ktmp[:], in0=ktmp[:], in1=cos_sb[:], op=ALU.mult), "sv")
                                cx.inc(v.tensor_tensor(out=rot[:], in0=rot[:], in1=sinp_sb[:], op=ALU.mult), "sv")
                                vbar(v)
                                cx.inc(v.tensor_tensor(out=kro[:], in0=ktmp[:], in1=rot[:], op=ALU.add), "sv")
                            kron = cx.n["sv"]
    
                            @block.sync
                            def _(sync, et=et, kron=kron):
                                sync.wait_ge(cx.h["sv"], kron)
                                cx.inc(sync.dma_start(out=ccK_i[et * 128:(et + 1) * 128, :], in_=kro[:]), "sd", 16)
                        k_done = dict(cx.n)
    
                        wn = cast_w(wuv, 2048, "uv")
                        vloc = sb(s8, "b_vloc", [128, 2080], BF16)
                        for tokt in range(NT):
                            @block.vector
                            def _(v, tokt=tokt):
                                if tokt == 0:
                                    v.wait_ge(cx.h["sv"], wn)
                                v.wait_ge(cx.h["sd"], cx.n["sd"])
                                cx.inc(v.memset(vloc[:].rearrange("p (h e) -> p h e", e=130)[:, :, 128:130], 1.0), "sv")
                            vz = cx.n["sv"]
    
                            @block.tensor
                            def _(te, tokt=tokt, vz=vz):
                                te.wait_ge(cx.h["sv"], vz)
                                if tokt >= 1:
                                    te.wait_ge(cx.h["ss"], cx.n["ss"])
                                for hc in range(4):
                                    for ct in range(4):
                                        mm = te.matmul(out=pj[4 + hc][:],
                                                       lhsT=c_kvT[:, ct * 1024 + tokt * 128: ct * 1024 + tokt * 128 + 128],
                                                       rhs=wbf[:, ct * 2048 + hc * 512:(ct * 2048) + (hc + 1) * 512],
                                                       start=(ct == 0), stop=(ct == 3))
                                        if ct == 3:
                                            cx.inc(mm, "stt")
                            vtn = cx.n["stt"]
    
                            @block.scalar
                            def _(sc, tokt=tokt, vtn=vtn):
                                sc.wait_ge(cx.h["stt"], vtn)
                                for hc in range(4):
                                    for hh in range(4):
                                        cx.inc(sc.copy(
                                            out=vloc[:, (hc * 4 + hh) * 130:(hc * 4 + hh) * 130 + 128],
                                            in_=pj[4 + hc][:, hh * 128:(hh + 1) * 128]), "ss")
                            vevn = cx.n["ss"]
    
                            @block.sync
                            def _(sync, tokt=tokt, vevn=vevn):
                                sync.wait_ge(cx.h["ss"], vevn)
                                cx.inc(sync.dma_start(out=ccV_i[tokt * 128:(tokt + 1) * 128, :], in_=vloc[:]), "sd", 16)
                        v_done = dict(cx.n)
    
                        wn = cast_w(wdq, 1024, "dq")
                        for cqt in range(8):
                            @block.tensor
                            def _(te, cqt=cqt, wn=wn):
                                te.wait_ge(cx.h["sv"], wn)
                                if cqt >= 2:
                                    te.wait_ge(cx.h["ss"], cx.n["ss"])
                                for ch in range(2):
                                    for ds in range(NDT):
                                        mm = te.matmul(out=pj[cqt % 2 * 2 + ch][:],
                                                       lhsT=wbf[:, ds * 1024 + cqt * 128: ds * 1024 + cqt * 128 + 128],
                                                       rhs=xlnT[:, ds * 1024 + ch * 512: ds * 1024 + (ch + 1) * 512],
                                                       start=(ds == 0), stop=(ds == NDT - 1))
                                        if ds == NDT - 1:
                                            cx.inc(mm, "stt")
                            cqn = cx.n["stt"]
    
                            @block.scalar
                            def _(sc, cqt=cqt, cqn=cqn):
                                sc.wait_ge(cx.h["stt"], cqn)
                                for ch in range(2):
                                    cx.inc(sc.copy(out=c_qT[:, cqt * 1024 + ch * 512: cqt * 1024 + (ch + 1) * 512],
                                                   in_=pj[cqt % 2 * 2 + ch][:]), "ss")
                        cq_done = dict(cx.n)
    
                        for qe in range(32):
                            wq = dict(cx.n)
    
                            @block.sync
                            def _(sync, qe=qe, wq=wq):
                                if qe >= 2:
                                    sync.wait_ge(cx.h["stt"], wq["stt"])
                                cx.inc(sync.dma_start(
                                    out=wstg[:, (qe % 2) * 1024:(qe % 2) * 1024 + 1024]
                                    .rearrange("p (g w) -> p g w", w=128),
                                    in_=wuq[:, qe * 128:(qe + 1) * 128].rearrange("(g p) w -> p g w", p=128)),
                                    "sd", 16)
                            qldn = cx.n["sd"]
    
                            @block.vector
                            def _(v, qe=qe, qldn=qldn):
                                v.wait_ge(cx.h["sd"], qldn)
                                if qe >= 2:
                                    v.wait_ge(cx.h["stt"], cx.n["stt"])
                                cx.inc(v.tensor_copy(out=wbf[:, (qe % 2) * 1024:(qe % 2) * 1024 + 1024],
                                                     in_=wstg[:, (qe % 2) * 1024:(qe % 2) * 1024 + 1024]), "sv")
                            qcn = cx.n["sv"]
    
                            @block.tensor
                            def _(te, qe=qe, qcn=qcn):
                                te.wait_ge(cx.h["sv"], qcn)
                                if qe == 0:
                                    te.wait_ge(cx.h["ss"], cq_done["ss"])
                                if qe >= 2:
                                    te.wait_ge(cx.h["ss"], cx.n["ss"])
                                for ch in range(2):
                                    for cq in range(8):
                                        mm = te.matmul(out=pj[qe % 2 * 2 + ch][:],
                                                       lhsT=wbf[:, (qe % 2) * 1024 + cq * 128:(qe % 2) * 1024 + cq * 128 + 128],
                                                       rhs=c_qT[:, cq * 1024 + ch * 512: cq * 1024 + (ch + 1) * 512],
                                                       start=(cq == 0), stop=(cq == 7))
                                        if cq == 7:
                                            cx.inc(mm, "stt")
                            qtn = cx.n["stt"]
    
                            @block.scalar
                            def _(sc, qe=qe, qtn=qtn):
                                sc.wait_ge(cx.h["stt"], qtn)
                                sc.wait_ge(cx.h["sg"], cx.n["sg"])
                                for ch in range(2):
                                    cx.inc(sc.copy(out=ktmp[:, ch * 512:(ch + 1) * 512],
                                                   in_=pj[qe % 2 * 2 + ch][:]), "ss")
                            qevn = cx.n["ss"]
    
                            @block.gpsimd
                            def _(g, qevn=qevn):
                                g.wait_ge(cx.h["ss"], qevn)
                                cx.inc(g.dma_start(out=rot[0:64, :], in_=ktmp[64:128, :]), "sg", 16)
                                cx.inc(g.dma_start(out=rot[64:128, :], in_=ktmp[0:64, :]), "sg", 16)
                            rotn = cx.n["sg"]
    
                            @block.vector
                            def _(v, qe=qe, rotn=rotn, qevn=qevn):
                                v.wait_ge(cx.h["ss"], qevn)
                                v.wait_ge(cx.h["sg"], rotn)
                                v.wait_ge(cx.h["sv"], cx.n["sv"])
                                cx.inc(v.tensor_tensor(out=ktmp[:], in0=ktmp[:], in1=cos_sb[:], op=ALU.mult), "sv")
                                cx.inc(v.tensor_tensor(out=rot[:], in0=rot[:], in1=sinp_sb[:], op=ALU.mult), "sv")
                                vbar(v)
                                cx.inc(v.tensor_tensor(out=kro[:], in0=ktmp[:], in1=rot[:], op=ALU.add), "sv")
                            qron = cx.n["sv"]
    
                            @block.sync
                            def _(sync, qe=qe, qron=qron):
                                sync.wait_ge(cx.h["sv"], qron)
                                cx.inc(sync.dma_start(out=q_loc[qe * 128:(qe + 1) * 128, :], in_=kro[:]), "sd", 16)
                        q_done = dict(cx.n)
    
                        wn = cast_w(wlam, NH, "lam")
                        for tokt in range(NT):
                            @block.tensor
                            def _(te, tokt=tokt, wn=wn):
                                te.wait_ge(cx.h["sv"], wn)
                                if tokt >= 2:
                                    te.wait_ge(cx.h["sv"], cx.n["sv"])
                                for ds in range(NDT):
                                    mm = te.matmul(out=pj[tokt % 2][:, 0:NH],
                                                   lhsT=xlnT[:, ds * 1024 + tokt * 128: ds * 1024 + tokt * 128 + 128],
                                                   rhs=wbf[:, ds * NH:(ds + 1) * NH],
                                                   start=(ds == 0), stop=(ds == NDT - 1))
                                    if ds == NDT - 1:
                                        cx.inc(mm, "stt")
                            lmn = cx.n["stt"]
    
                            @block.vector
                            def _(v, tokt=tokt, lmn=lmn):
                                v.wait_ge(cx.h["stt"], lmn)
                                cx.inc(v.tensor_tensor(out=lam_sb[:, tokt * NH:(tokt + 1) * NH],
                                                       in0=pj[tokt % 2][:, 0:NH], in1=blam_sb[:], op=ALU.add), "sv")
                            lan = cx.n["sv"]
    
                            @block.scalar
                            def _(sc, tokt=tokt, lan=lan):
                                sc.wait_ge(cx.h["sv"], lan)
                                cx.inc(sc.activation(out=lam_sb[:, tokt * NH:(tokt + 1) * NH],
                                                     in_=lam_sb[:, tokt * NH:(tokt + 1) * NH],
                                                     func=ACT.Sigmoid), "ss")
                        proj_done = dict(cx.n)
    
                phase_barrier()
                attn_flat = sb(s6, "b_af", [128, NT * D], BF16)

                @block.gpsimd
                def _(g):
                    g.wait_ge(cx.h["sd"], proj_done["sd"])
                    cx.inc(g.collective_compute("AllGather", ALU.bypass, replica_groups=GRP4,
                                                ins=[ccK_i], outs=[ccK_m]), "scc")
                    cx.wait(g, "scc")
                    cx.inc(g.collective_compute("AllGather", ALU.bypass, replica_groups=GRP2,
                                                ins=[ccK_m], outs=[K_g]), "scc")
                    cx.wait(g, "scc")
                    cx.inc(g.collective_compute("AllGather", ALU.bypass, replica_groups=GRP4,
                                                ins=[ccV_i], outs=[ccV_m]), "scc")
                    cx.wait(g, "scc")
                    cx.inc(g.collective_compute("AllGather", ALU.bypass, replica_groups=GRP2,
                                                ins=[ccV_m], outs=[V_g]), "scc")
                kv_ready = cx.n["scc"]

                with ExitStack() as s9:
                    exp0 = sb(s9, "b_exp0", [128, 32 * 512], BF16)
                    exp1 = sb(s9, "b_exp1", [128, 32 * 512], BF16)
                    qb = [sb(s9, "b_qb%d" % i, [128, 1024], BF16) for i in range(4)]
                    kbuf = [sb(s9, "b_kb%d" % i, [128, 128], BF16) for i in range(2)]
                    vbuf = [sb(s9, "b_vb%d" % i, [128, 130], BF16) for i in range(2)]
                    r1 = sb(s9, "b_r1", [128, 1], F32)
                    r2 = sb(s9, "b_r2", [128, 1], F32)
                    s2n = sb(s9, "b_s2n", [128, 1], F32)
                    tmt = sb(s9, "b_tmt", [128, 128], F32)
                    dbsc = sb(s9, "b_dbsc", [128, 258], F32)
                    ps_s = [ps(s9, "b_ps%d" % i, [128, 512], F32) for i in range(4)]
                    pv = [[ps(s9, "b_pv%d_%d" % (i, j), [128, 512], F32) for j in range(2)]
                          for i in range(2)]

                    def pv_ap(p_, sl, c0, c1):
                        return pv[p_][sl // 2][:, (sl % 2) * 256 + c0:(sl % 2) * 256 + c1]
                    exps = [exp0, exp1]

                    for b in range(NB):
                        for h in range(NH):
                            ih = b * NH + h
                            qn0 = dict(cx.n)

                            @block.sync
                            def _(sync, b=b, h=h, ih=ih, qn0=qn0):
                                if b == 0 and h == 0:
                                    sync.wait_ge(cx.h["scc"], kv_ready)
                                if ih >= 2:
                                    sync.wait_ge(cx.h["stt"], qn0["stt"])
                                for p_ in range(2):
                                    cx.inc(sync.dma_start(out=qb[(ih % 2) * 2 + p_][:],
                                                          in_=q_loc[(2 * h + p_) * 128:(2 * h + p_ + 1) * 128, :]),
                                           "sd", 16)
                            qldn = cx.n["sd"]

                            ev_start = cx.n["sv"]
                            exp_marks = {}
                            mask_marks = {}
                            smm_marks = {}
                            kld = []
                            vld = []
                            for kt in range(32):
                                g_ = kt // 8
                                N_ = (4 - g_) * 128
                                c8, sk = kt % 8, kt // 8

                                @block.sync
                                def _(sync, b=b, h=h, kt=kt, c8=c8, sk=sk):
                                    if kt >= 2:
                                        sync.wait_ge(cx.h["stt"], smm_marks[kt - 2])
                                    cx.inc(sync.dma_start(
                                        out=kbuf[kt % 2][:],
                                        in_=K_g[c8 * 2048 + h * 128: c8 * 2048 + h * 128 + 128,
                                                (b * 4 + sk) * 128:(b * 4 + sk + 1) * 128]), "sd", 16)
                                    if kt >= 2:
                                        sync.wait_ge(cx.h["stt"], cx.n["stt"])
                                    cx.inc(sync.dma_start(
                                        out=vbuf[kt % 2][:],
                                        in_=V_g[c8 * 1024 + (b * 4 + sk) * 128: c8 * 1024 + (b * 4 + sk + 1) * 128,
                                                h * 130:(h + 1) * 130]), "sd", 16)
                                kvldn = cx.n["sd"]

                                @block.tensor
                                def _(te, kt=kt, g_=g_, N_=N_, b=b, ih=ih, kvldn=kvldn, qldn=qldn, ev_start=ev_start):
                                    te.wait_ge(cx.h["sd"], kvldn - 16)
                                    if kt == 0:
                                        te.wait_ge(cx.h["sd"], qldn)
                                        te.wait_ge(cx.h["sv"], ev_start)
                                    if kt >= 2:
                                        te.wait_ge(cx.h["ss"], exp_marks[kt - 2])
                                    for p_ in range(2):
                                        cx.inc(te.matmul(out=ps_s[(kt % 2) * 2 + p_][:, 0:N_],
                                                         lhsT=kbuf[kt % 2][:],
                                                         rhs=qb[(ih % 2) * 2 + p_][:, b * 512 + g_ * 128: b * 512 + 512],
                                                         start=True, stop=True), "stt")
                                smm_marks[kt] = cx.n["stt"]

                                @block.scalar
                                def _(sc, kt=kt, g_=g_, N_=N_):
                                    sc.wait_ge(cx.h["stt"], smm_marks[kt])
                                    for p_ in range(2):
                                        cx.inc(sc.activation(out=exps[p_][:, kt * 512: kt * 512 + N_],
                                                             in_=ps_s[(kt % 2) * 2 + p_][:, 0:N_],
                                                             func=ACT.Exp, scale=SCALE_ATT), "ss")
                                exp_marks[kt] = cx.n["ss"]

                                @block.vector
                                def _(v, kt=kt):
                                    v.wait_ge(cx.h["ss"], exp_marks[kt])
                                    for p_ in range(2):
                                        cx.inc(v.tensor_tensor(out=exps[p_][:, kt * 512: kt * 512 + 128],
                                                               in0=exps[p_][:, kt * 512: kt * 512 + 128],
                                                               in1=mask_sb[:, (kt % 8) * 128:(kt % 8) * 128 + 128],
                                                               op=ALU.mult), "sv")
                                mask_marks[kt] = cx.n["sv"]
                                if b == 0 and h == 0 and kt == 0:
                                    @block.gpsimd
                                    def _(g, kvldn=kvldn):
                                        g.wait_ge(cx.h["sv"], mask_marks[0])
                                        g.wait_ge(cx.h["sd"], kvldn)
                                        cx.inc(g.dma_start(out=dbg[1, :, 0:512], in_=exp0[:, 0:512]), "sg", 16)
                                        cx.inc(g.dma_start(out=dbg[1, :, 512:640], in_=kbuf[0][:]), "sg", 16)
                                        cx.inc(g.dma_start(out=dbg[1, :, 640:770], in_=vbuf[0][:]), "sg", 16)
                                        cx.inc(g.dma_start(out=dbg[2], in_=qb[0][:]), "sg", 16)

                                @block.tensor
                                def _(te, kt=kt, g_=g_, kvldn=kvldn):
                                    te.wait_ge(cx.h["sv"], mask_marks[kt])
                                    te.wait_ge(cx.h["sd"], kvldn)
                                    for p_ in range(2):
                                        for sl in range(g_, 4):
                                            out_ap = pv_ap(p_, sl, 0, 130)
                                            mm = te.matmul(out=out_ap,
                                                           lhsT=exps[p_][:, kt * 512 + (sl - g_) * 128: kt * 512 + (sl - g_) * 128 + 128],
                                                           rhs=vbuf[kt % 2][:],
                                                           start=(kt == 0 and sl % 2 == 0),
                                                           stop=(kt == 8 * sl + 7))
                                            if kt == 8 * sl + 7 and p_ == 1:
                                                cx.inc(mm, "stt")


                            pvn = cx.n["stt"]
                            for sl in range(4):
                                @block.vector
                                def _(v, b=b, h=h, sl=sl, pvn=pvn):
                                    v.wait_ge(cx.h["stt"], pvn)
                                    att0 = pv_ap(0, sl, 0, 128)
                                    den0 = pv_ap(0, sl, 128, 129)
                                    att1 = pv_ap(1, sl, 0, 128)
                                    den1 = pv_ap(1, sl, 128, 129)
                                    cx.inc(v.reciprocal(out=r1[:], in_=den0), "sv")
                                    cx.inc(v.reciprocal(out=r2[:], in_=den1), "sv")
                                    vbar(v)
                                    cx.inc(v.tensor_scalar(out=tmt[:], in0=att0, scalar1=r1[:],
                                                           scalar2=None, op0=ALU.mult), "sv")
                                    cx.inc(v.tensor_scalar(out=s2n[:], in0=lam_sb[:, (b * 4 + sl) * NH + h:
                                                                                  (b * 4 + sl) * NH + h + 1],
                                                           scalar1=r2[:], scalar2=-1.0,
                                                           op0=ALU.mult, op1=ALU.mult), "sv")
                                    vbar(v)
                                    cx.inc(v.scalar_tensor_tensor(
                                        out=attn_flat[:, (b * 4 + sl) * D + h * 128:(b * 4 + sl) * D + h * 128 + 128],
                                        in0=att1, scalar=s2n[:], in1=tmt[:],
                                        op0=ALU.mult, op1=ALU.add), "sv")
                                    if b == 0 and h == 0 and sl == 0:
                                        v.tensor_copy(out=dbsc[:, 0:128], in_=att0)
                                        v.tensor_copy(out=dbsc[:, 128:129], in_=den0)
                                        v.tensor_copy(out=dbsc[:, 129:130], in_=den1)
                                        v.tensor_copy(out=dbsc[:, 130:131], in_=r1[:])
                                        v.tensor_copy(out=dbsc[:, 131:132], in_=s2n[:])
                                        cx.inc(v.tensor_copy(out=dbsc[:, 132:258], in_=tmt[:, 0:126]), "sv")
                    att_done = dict(cx.n)

                    @block.vector
                    def _(v):
                        v.wait_ge(cx.h["sv"], att_done["sv"])
                        v.tensor_copy(out=dbsc[:, 0:258], in_=pv[0][0][:, 0:258])
                        cx.inc(v.tensor_copy(out=tmt[:], in_=exp0[:, 0:128]), "sv")
                    dbn2 = cx.n["sv"]

                    @block.gpsimd
                    def _(g, dbn2=dbn2):
                        g.wait_ge(cx.h["sv"], dbn2)
                        cx.inc(g.dma_start(out=dbg[0], in_=attn_flat[:, 0:1024]), "sg", 16)
                        cx.inc(g.dma_start(out=dbg[3, :, 0:128], in_=lam_sb[:]), "sg", 16)
                        cx.inc(g.dma_start(out=dbg[3, :, 128:386], in_=dbsc[:]), "sg", 16)
                        cx.inc(g.dma_start(out=dbg[3, :, 386:514], in_=tmt[:]), "sg", 16)
                phase_barrier()

                with ExitStack() as sA:
                    woutb = sb(sA, "b_wo", [128, NDT * 2048], BF16)
                    wost = sb(sA, "b_wos", [128, 2048], F32)
                    attT = sb(sA, "b_attT", [128, NDT * 128], BF16)
                    ptE = ps(sA, "b_ptE", [128, 128], BF16)
                    ptF = ps(sA, "b_ptF", [128, 128], BF16)
                    po = [ps(sA, "b_po%d" % i, [128, 512], F32) for i in range(4)]
                    pipe5 = TrPipe(cx, ptE, ptF, ident_bf, "ss")

                    for ds in range(NDT):
                        wl = dict(cx.n)

                        @block.sync
                        def _(sync, ds=ds, wl=wl):
                            if ds >= 1:
                                sync.wait_ge(cx.h["sv"], wl["sv"])
                            cx.inc(sync.dma_start(out=wost[:], in_=wout[ds * 128:(ds + 1) * 128, :]), "sd", 16)
                        wln = cx.n["sd"]

                        @block.vector
                        def _(v, ds=ds, wln=wln):
                            v.wait_ge(cx.h["sd"], wln)
                            cx.inc(v.tensor_copy(out=woutb[:, ds * 2048:(ds + 1) * 2048], in_=wost[:]), "sv")
                    wo_done = dict(cx.n)

                    for tokt in range(NT):
                        def evt(eng, i, psum_ap, tokt=tokt):
                            return eng.copy(out=attT[:, i * 128:(i + 1) * 128], in_=psum_ap)
                        pre = [("sv", att_done["sv"])] if tokt == 0 else None
                        pipe5.group([attn_flat[:, tokt * D + dt * 128: tokt * D + dt * 128 + 128]
                                     for dt in range(NDT)],
                                    evt, "scalar", pre_waits=pre,
                                    per_tr_wait=(lambda i, w=dict(cx.n): (("stt", w["stt"]) if i == 0 and tokt > 0 else None)))
                        trn = dict(cx.n)

                        @block.tensor
                        def _(te, tokt=tokt, trn=trn):
                            te.wait_ge(cx.h["ss"], trn["ss"])
                            if tokt == 0:
                                te.wait_ge(cx.h["sv"], wo_done["sv"])
                            if tokt >= 2:
                                te.wait_ge(cx.h["sv"], cx.n["sv"])
                            for ds in range(NDT):
                                for oc in range(4):
                                    mm = te.matmul(out=po[oc][:],
                                                   lhsT=attT[:, ds * 128:(ds + 1) * 128],
                                                   rhs=woutb[:, ds * 2048 + oc * 512: ds * 2048 + (oc + 1) * 512],
                                                   start=(ds == 0), stop=(ds == NDT - 1))
                                    if ds == NDT - 1:
                                        cx.inc(mm, "stt")
                        opn = cx.n["stt"]

                        @block.vector
                        def _(v, tokt=tokt, opn=opn):
                            v.wait_ge(cx.h["stt"], opn)
                            for oc in range(4):
                                cx.inc(v.tensor_tensor(
                                    out=x_sb[:, tokt * D + oc * 512: tokt * D + (oc + 1) * 512],
                                    in0=po[oc][:],
                                    in1=x_sb[:, tokt * D + oc * 512: tokt * D + (oc + 1) * 512],
                                    op=ALU.add), "sv")

        if phases >= 3:
            phase_barrier()

        if phases >= 4:
            emit_ffn_outer(1)

        @block.sync
        def _(sync):
            cx.wait(sync, "sv")
            cx.wait(sync, "ss")
            cx.wait(sync, "stt")
            cx.wait(sync, "scc")
            for t in range(NT):
                cx.inc(sync.dma_start(out=out8[t], in_=xt(t)), "sd", 16)
            cx.wait(sync, "sd")

    return nc


def make_inputs(full, c):
    blocks = [8 * s + c for s in range(4)]
    x = np.asarray(full["L0_attn_a_x"], dtype=np.float32)
    x8 = np.stack([x[b, blk * 128:(blk + 1) * 128, :] for b in range(NB) for blk in blocks])
    W = np.asarray(full["L0_attn_a_W"], np.float32)
    bvec = np.asarray(full["L0_attn_a_b"], np.float32)
    gate = np.asarray(full["L0_ffn_gate_w"], np.float32)
    up = np.asarray(full["L0_ffn_up_w"], np.float32)
    down = np.asarray(full["L0_ffn_down_w"], np.float32)

    pos = np.concatenate([np.arange(blk * 128, blk * 128 + 128) for blk in blocks]).astype(np.float64)
    j = np.arange(64, dtype=np.float64)
    inv = 1.0 / (10000.0 ** (2 * j / 128.0))
    ang = pos[None, :] * inv[:, None]
    cos_h = np.cos(ang).astype(np.float32)
    sin_h = np.sin(ang).astype(np.float32)

    mask8 = np.zeros((8, 128, 128), np.float32)
    for jj in range(8):
        if jj < c:
            mask8[jj] = 1.0
        elif jj == c:
            mask8[jj] = np.triu(np.ones((128, 128), np.float32))
    sel4 = np.zeros((4, 128, 32), np.float32)
    for s in range(4):
        sel4[s, :, 8 * s + c] = 1.0

    return dict(
        x8=np.ascontiguousarray(x8),
        wa_sl=np.ascontiguousarray(W[:, 256 * c:256 * c + 256]),
        b_col=np.ascontiguousarray(bvec.reshape(NDT, 128).T),
        gate_sl=np.ascontiguousarray(gate[:, 256 * c:256 * c + 256]),
        up_sl=np.ascontiguousarray(up[:, 256 * c:256 * c + 256]),
        down_sl=np.ascontiguousarray(down[:, 1024 * c:1024 * c + 1024]),
        wdkv=np.ascontiguousarray(np.asarray(full["L0_attn_b_W_DKV"], np.float32)),
        wuk=np.ascontiguousarray(np.asarray(full["L0_attn_b_W_UK"], np.float32)),
        wuv=np.ascontiguousarray(np.asarray(full["L0_attn_b_W_UV"], np.float32)),
        wdq=np.ascontiguousarray(np.asarray(full["L0_attn_b_W_DQ"], np.float32)),
        wuq=np.ascontiguousarray(np.asarray(full["L0_attn_b_W_UQ"], np.float32)),
        wlam=np.ascontiguousarray(np.asarray(full["L0_attn_b_W_lam"], np.float32)),
        wout=np.ascontiguousarray(np.asarray(full["L0_attn_b_W_out"], np.float32)),
        blam_rep=np.ascontiguousarray(np.broadcast_to(
            np.asarray(full["L0_attn_b_b_lam"], np.float32)[None, :], (128, NH))),
        cos_t=np.ascontiguousarray(np.tile(np.concatenate([cos_h, cos_h], 0), (1, 2))),
        sinp_t=np.ascontiguousarray(np.tile(np.concatenate([-sin_h, sin_h], 0), (1, 2))),
        mask8=np.ascontiguousarray(mask8.transpose(1, 0, 2).reshape(128, 1024)).astype(ml_dtypes.bfloat16),
        sel4=np.ascontiguousarray(sel4.transpose(1, 0, 2).reshape(128, 128)),
        id_bf=np.eye(128, dtype=ml_dtypes.bfloat16),
        id_f32=np.eye(128, dtype=np.float32),
    )


def kernel(**inputs):
    nc = build()
    in_maps = [make_inputs(inputs, c) for c in range(8)]
    res = run_bass_kernel_spmd(nc, in_maps, core_ids=list(range(8)))
    out = np.zeros((NB, T, D), dtype=np.float32)
    for c in range(8):
        o = np.asarray(res.results[c]["out8"])
        i = 0
        for b in range(NB):
            for s in range(4):
                blk = 8 * s + c
                out[b, blk * 128:(blk + 1) * 128, :] = o[i]
                i += 1
    return out
